# version 06
# baseline (speedup 1.0000x reference)
import numpy as np
import ml_dtypes
import concourse.bass as bass
import concourse.mybir as mybir
from concourse.bass_utils import run_bass_kernel_spmd

F32 = mybir.dt.float32
BF16 = mybir.dt.bfloat16
OP = mybir.AluOpType
AX = mybir.AxisListType.X
AF = mybir.ActivationFunctionType

SEQ = 2048
D = 1024
NH = 64
II = 2816
T = SEQ // 128
C = D // 128
IC = II // 128
MAGIC = 12582912.0
EPS = 1e-5
HALF = 1024
NCORES = 8
NHEADS = 2 * NH
STOP = "all"


def _finish(nc, x_tok, out_hbm, sd_x, n_):
    with nc.Block() as blk:
        @blk.sync
        def _(sync):
            for t in range(T):
                sync.dma_start(out=out_hbm[t * 128:(t + 1) * 128, :],
                               in_=x_tok.ap()[:, t * D:(t + 1) * D]
                               ).then_inc(sd_x, 16)
            sync.wait_ge(sd_x, n_["dx"] + 16 * T)


def _attn_heads():
    heads = []
    for ih in range(2):
        for h in range(NH):
            chunks = []
            for jc in range(8 if ih == 0 else 16):
                i_lo = max(jc * 128 - ih * HALF, 0)
                diag = (jc * 128) >= ih * HALF
                chunks.append((jc, i_lo, diag))
            heads.append((ih * NH + h, ih, h, chunks))
    return heads


def _nd_pieces(i_lo):
    return [(p, max(i_lo, 512 * p), 512 * (p + 1))
            for p in range(2) if max(i_lo, 512 * p) < 512 * (p + 1)]


def build_nc():
    nc = bass.Bass("TRN2", target_bir_lowering=False,
                   dynamic_dma_scratch_size=512)

    x_hbm = nc.dram_tensor("x", [SEQ, D], F32, kind="ExternalInput").ap()
    wqkv_hbm = nc.dram_tensor("w_qkv", [192, D], F32, kind="ExternalInput").ap()
    wout_hbm = nc.dram_tensor("w_out", [D, NH], F32, kind="ExternalInput").ap()
    clipx_hbm = nc.dram_tensor("clip_x", [1], F32, kind="ExternalInput").ap()
    clipd_hbm = nc.dram_tensor("clip_d", [1], F32, kind="ExternalInput").ap()
    wg_hbm = nc.dram_tensor("w_gate", [II, D], F32, kind="ExternalInput").ap()
    wu_hbm = nc.dram_tensor("w_up", [II, D], F32, kind="ExternalInput").ap()
    wd_hbm = nc.dram_tensor("w_down", [D, II], F32, kind="ExternalInput").ap()
    out_hbm = nc.dram_tensor("out", [SEQ, D], F32, kind="ExternalOutput").ap()

    cc_in1 = nc.dram_tensor("cc_in1", [1, 2048], F32)
    cc_out1 = nc.dram_tensor("cc_out1", [1, 2048], F32)
    cc_in2 = nc.dram_tensor("cc_in2", [1, 2048], F32)
    cc_out2 = nc.dram_tensor("cc_out2", [1, 2048], F32)

    tri_c = nc.inline_tensor(
        (np.arange(128)[None, :] >= np.arange(128)[:, None]
         ).astype(ml_dtypes.bfloat16), name="tri_c")
    idf_c = nc.inline_tensor(np.eye(128, dtype=np.float32), name="idf_c")
    idb_c = nc.inline_tensor(np.eye(128).astype(ml_dtypes.bfloat16), name="idb_c")
    sel_c = nc.inline_tensor(
        np.kron(np.eye(64), np.ones((1, 128))).astype(ml_dtypes.bfloat16),
        name="sel_c")
    ones_c = nc.inline_tensor(np.ones((1, 128), dtype=np.float32), name="ones_c")

    from contextlib import ExitStack
    with ExitStack() as ctx:
        en = ctx.enter_context

        x_tok = en(nc.sbuf_tensor("x_tok", [128, T * D], F32))
        tri = en(nc.sbuf_tensor("tri", [128, 128], BF16))
        idf = en(nc.sbuf_tensor("idf", [128, 128], F32))
        idb = en(nc.sbuf_tensor("idb", [128, 128], BF16))
        ones1 = en(nc.sbuf_tensor("ones1", [1, 128], F32))
        clipx_r = en(nc.sbuf_tensor("clipx_r", [128, 1], F32))
        clipd_r = en(nc.sbuf_tensor("clipd_r", [128, 1], F32))
        mu_all = en(nc.sbuf_tensor("mu_all", [128, T], F32))
        rstd_all = en(nc.sbuf_tensor("rstd_all", [128, T], F32))
        amax_run = en(nc.sbuf_tensor("amax_run", [128, 1], F32))
        sc_sx = en(nc.sbuf_tensor("sc_sx", [128, 1], F32))
        sc_rx = en(nc.sbuf_tensor("sc_rx", [128, 1], F32))
        sc_sh = en(nc.sbuf_tensor("sc_sh", [128, 1], F32))
        sc_rh = en(nc.sbuf_tensor("sc_rh", [128, 1], F32))
        gmax_r = en(nc.sbuf_tensor("gmax_r", [128, 1], F32))
        tmp_a = en(nc.sbuf_tensor("tmp_a", [128, 1], F32))
        tmp_b = en(nc.sbuf_tensor("tmp_b", [128, 1], F32))
        tmp_c = en(nc.sbuf_tensor("tmp_c", [128, 1], F32))
        tmp_d = en(nc.sbuf_tensor("tmp_d", [128, 1], F32))
        tmp_e = en(nc.sbuf_tensor("tmp_e", [128, 1], F32))
        tmp_f = en(nc.sbuf_tensor("tmp_f", [128, 1], F32))
        amat = en(nc.sbuf_tensor("amat", [128, T], F32))
        bmat = en(nc.sbuf_tensor("bmat", [128, T], F32))
        scr1 = en(nc.sbuf_tensor("scr1", [128, 1024], F32))
        eps_r = en(nc.sbuf_tensor("eps_r", [128, 1], F32))

        sd_x = en(nc.semaphore("sd_x"))
        sd_w = en(nc.semaphore("sd_w"))
        s_T = en(nc.semaphore("s_T"))
        s_S = en(nc.semaphore("s_S"))
        s_V = en(nc.semaphore("s_V"))
        s_cc = en(nc.semaphore("s_cc"))
        sd_p0 = en(nc.semaphore("sd_p0"))
        sd_p1 = en(nc.semaphore("sd_p1"))
        sd_wg0 = en(nc.semaphore("sd_wg0"))
        sd_wg1 = en(nc.semaphore("sd_wg1"))
        sd_wu0 = en(nc.semaphore("sd_wu0"))
        sd_wu1 = en(nc.semaphore("sd_wu1"))
        sd_wg = (sd_wg0, sd_wg1)
        sd_wu = (sd_wu0, sd_wu1)
        sd_tg = (en(nc.semaphore("sd_tg0")), en(nc.semaphore("sd_tg1")))
        sd_tu = (en(nc.semaphore("sd_tu0")), en(nc.semaphore("sd_tu1")))
        sd_tq = (en(nc.semaphore("sd_tq0")), en(nc.semaphore("sd_tq1")))
        sd_td = (en(nc.semaphore("sd_td0")), en(nc.semaphore("sd_td1")))
        sd_tl = (en(nc.semaphore("sd_tl0")), en(nc.semaphore("sd_tl1")))
        sd_wd = en(nc.semaphore("sd_wd"))
        n_ = {"dx": 0, "dw": 0, "T": 0, "S": 0, "V": 0, "cc": 0}

        with ExitStack() as actx:
            an = actx.enter_context
            sel = an(nc.sbuf_tensor("sel", [64, NH * 128], BF16))
            wqkvT = an(nc.sbuf_tensor("wqkvT", [128, C * 192], BF16))
            woutT = an(nc.sbuf_tensor("woutT", [64, D], BF16))
            qk_fm = an(nc.sbuf_tensor("qk_fm", [128, SEQ], F32))
            qb = an(nc.sbuf_tensor("qb", [64, SEQ], BF16))
            v_fm = an(nc.sbuf_tensor("v_fm", [64, SEQ], F32))
            kT = an(nc.sbuf_tensor("kT", [128, T * NH], F32))
            vT_p = an(nc.sbuf_tensor("vT_p", [128, T * 128], BF16))
            negm = an(nc.sbuf_tensor("negm", [128, NH], F32))
            negm_row = an(nc.sbuf_tensor("negm_row", [1, NH], F32))
            E_buf = an(nc.sbuf_tensor("E_buf", [128, 4 * HALF], BF16))
            pair_sb = an(nc.sbuf_tensor("pair_sb", [2, 2 * HALF], F32))
            na_sb = an(nc.sbuf_tensor("na_sb", [64, SEQ], F32))
            da_sb = an(nc.sbuf_tensor("da_sb", [64, SEQ], F32))
            dr_sb = an(nc.sbuf_tensor("dr_sb", [64, SEQ], F32))
            attn_o = an(nc.sbuf_tensor("attn_o", [64, SEQ], BF16))
            ln_tile = an(nc.sbuf_tensor("ln_tile", [128, 2 * D], BF16))
            ln1T = an(nc.sbuf_tensor("ln1T", [128, C * SEQ], BF16))

            with (
                nc.sbuf_tensor("wq_a", [128, D], F32) as wq_a,
                nc.sbuf_tensor("wq_b", [64, D], F32) as wq_b,
                nc.sbuf_tensor("wo_raw", [128, C * NH], F32) as wo_raw,
                nc.psum_tensor("p_t0", [128, 2048], F32) as p_t0,
                nc.Block() as blk,
            ):
                @blk.sync
                def _(sync):
                    for t in range(T):
                        sync.dma_start(out=x_tok.ap()[:, t * D:(t + 1) * D],
                                       in_=x_hbm[t * 128:(t + 1) * 128, :]
                                       ).then_inc(sd_x, 16)
                        sync.wait_ge(sd_x, (t + 1) * 16)
                n_["dx"] += T * 16

                @blk.scalar
                def _(g):
                    for dst, src in ((tri, tri_c), (idf, idf_c), (idb, idb_c),
                                     (ones1, ones_c), (sel, sel_c)):
                        g.dma_start(out=dst[:, :], in_=src.ap()).then_inc(sd_w, 16)
                    g.dma_start(out=clipx_r[:, :],
                                in_=clipx_hbm[None, :].to_broadcast([128, 1])
                                ).then_inc(sd_w, 16)
                    g.dma_start(out=clipd_r[:, :],
                                in_=clipd_hbm[None, :].to_broadcast([128, 1])
                                ).then_inc(sd_w, 16)
                    g.dma_start(out=wq_a[:, :], in_=wqkv_hbm[0:128, :]
                                ).then_inc(sd_w, 16)
                    g.dma_start(out=wq_b[:, :], in_=wqkv_hbm[128:192, :]
                                ).then_inc(sd_w, 16)
                    g.dma_start(out=wo_raw.ap().rearrange("p (c h) -> p c h", c=C),
                                in_=wout_hbm.rearrange("(c p) h -> p c h", p=128)
                                ).then_inc(sd_w, 16)
                n_["dw"] += 10 * 16

                items = ([("qk", kc) for kc in range(C)]
                         + [("v", kc) for kc in range(C)]
                         + [("wo", kc) for kc in range(C)])
                if STOP == "load":
                    items = []
                T0, V0 = n_["T"], n_["V"]

                @blk.tensor
                def _(tensor):
                    tensor.wait_ge(sd_w, n_["dw"])
                    for i, (kind, kc) in enumerate(items):
                        slot = (i % 4) * 512
                        if i >= 4:
                            tensor.wait_ge(s_V, V0 + (i - 4) + 1)
                        if kind == "qk":
                            tensor.transpose(p_t0[:, slot:slot + 128],
                                             wq_a[:, kc * 128:(kc + 1) * 128],
                                             idf[:, :]).then_inc(s_T, 1)
                        elif kind == "v":
                            tensor.transpose(p_t0[:, slot:slot + 64],
                                             wq_b[:, kc * 128:(kc + 1) * 128],
                                             idf[0:64, 0:64]).then_inc(s_T, 1)
                        else:
                            tensor.transpose(p_t0[0:64, slot:slot + 128],
                                             wo_raw[:, kc * NH:(kc + 1) * NH],
                                             idf[:, :]).then_inc(s_T, 1)
                n_["T"] += len(items)

                @blk.vector
                def _(v):
                    for i, (kind, kc) in enumerate(items):
                        slot = (i % 4) * 512
                        v.wait_ge(s_T, T0 + i + 1)
                        if kind == "qk":
                            v.tensor_copy(wqkvT.ap()[:, kc * 192:kc * 192 + 128],
                                          p_t0[:, slot:slot + 128]).then_inc(s_V, 1)
                        elif kind == "v":
                            v.tensor_copy(
                                wqkvT.ap()[:, kc * 192 + 128:kc * 192 + 192],
                                p_t0[:, slot:slot + 64]).then_inc(s_V, 1)
                        else:
                            v.tensor_copy(woutT.ap()[:, kc * 128:(kc + 1) * 128],
                                          p_t0[0:64, slot:slot + 128]).then_inc(s_V, 1)
                    v.memset(vT_p[:, :], 1.0)
                    v.memset(eps_r[:, :], EPS).then_inc(s_V, 1)
                n_["V"] += len(items) + 1

            nc.all_engine_barrier()
            if STOP in ("p0", "load"):
                _finish(nc, x_tok, out_hbm, sd_x, n_)
                return nc

            with nc.Block() as blk:
                V0, S0, T0 = n_["V"], n_["S"], n_["T"]

                def pos_var(t):
                    return V0 + 2 * t + 1

                def pos_ln(t):
                    return V0 + 2 * t + 2

                tl_after = {}
                tlc = [0, 0]
                for t in range(T):
                    tlc[t % 2] += 16 * C
                    tl_after[t] = tlc[t % 2]

                @blk.vector
                def _(v):
                    for t in range(T):
                        xt = x_tok.ap()[:, t * D:(t + 1) * D]
                        v.wait_ge(sd_x, (t + 1) * 16)
                        v.drain()
                        v.reduce_sum(tmp_a[:, :], xt, AX)
                        v.tensor_mul(scr1[:, :], xt, xt)
                        v.drain()
                        v.reduce_sum(tmp_b[:, :], scr1[:, :], AX)
                        v.drain()
                        v.tensor_scalar(mu_all.ap()[:, t:t + 1], tmp_a[:, :],
                                        1.0 / D, None, OP.mult)
                        v.tensor_scalar(tmp_b[:, :], tmp_b[:, :], 1.0 / D,
                                        None, OP.mult)
                        v.drain()
                        v.tensor_mul(tmp_c[:, :], mu_all.ap()[:, t:t + 1],
                                     mu_all.ap()[:, t:t + 1])
                        v.drain()
                        v.tensor_sub(tmp_d[:, :], tmp_b[:, :], tmp_c[:, :]
                                     ).then_inc(s_V, 1)
                        v.wait_ge(s_S, S0 + t + 1)
                        v.reciprocal(rstd_all.ap()[:, t:t + 1], tmp_a[:, :])
                        v.drain()
                        v.tensor_mul(tmp_c[:, :], mu_all.ap()[:, t:t + 1],
                                     rstd_all.ap()[:, t:t + 1])
                        v.drain()
                        v.tensor_scalar(tmp_c[:, :], tmp_c[:, :], -1.0, None,
                                        OP.mult)
                        v.drain()
                        if t >= 2:
                            v.wait_ge(sd_tl[t % 2], tl_after[t - 2])
                        v.tensor_scalar(
                            ln_tile.ap()[:, (t % 2) * D:(t % 2 + 1) * D], xt,
                            rstd_all.ap()[:, t:t + 1], tmp_c[:, :],
                            OP.mult, OP.add).then_inc(s_V, 1)
                n_["V"] += T * 2

                @blk.scalar
                def _(s):
                    for t in range(T):
                        s.wait_ge(s_V, pos_var(t))
                        s.activation(tmp_a[:, :], tmp_d[:, :], AF.Sqrt,
                                     bias=eps_r[:, :], scale=1.0).then_inc(s_S, 1)
                n_["S"] += T

                @blk.sync
                def _(sync):
                    for t in range(T):
                        sync.wait_ge(s_V, pos_ln(t))
                        for j in range(C):
                            sync.dma_start_transpose(
                                ln1T.ap()[:, j * SEQ + t * 128:
                                          j * SEQ + (t + 1) * 128],
                                ln_tile.ap()[:, (t % 2) * D + j * 128:
                                             (t % 2) * D + (j + 1) * 128]
                                ).then_inc(sd_tl[t % 2], 16)
                    sync.wait_ge(sd_tl[0], tlc[0])
                    sync.wait_ge(sd_tl[1], tlc[1])

            nc.all_engine_barrier()
            if STOP == "p1":
                _finish(nc, x_tok, out_hbm, sd_x, n_)
                return nc

            with (
                nc.psum_tensor("p_qkv", [128, SEQ], F32) as p_qkv,
                nc.psum_tensor("p_t2", [128, 1024], F32) as p_t2,
                nc.psum_tensor("p_sm", [1, 128], F32) as p_sm,
                nc.psum_tensor("p_sm2", [128, 64], F32) as p_sm2,
                nc.Block() as blk,
            ):
                T0, V0 = n_["T"], n_["V"]

                @blk.tensor
                def _(tensor):
                    for piece in range(4):
                        sl = slice(piece * 512, (piece + 1) * 512)
                        for kc in range(C):
                            mm = tensor.matmul(
                                p_qkv[:, sl],
                                wqkvT.ap()[:, kc * 192:kc * 192 + 128],
                                ln1T.ap()[:, kc * SEQ + piece * 512:
                                          kc * SEQ + (piece + 1) * 512],
                                start=(kc == 0), stop=(kc == C - 1))
                        mm.then_inc(s_T, 1)
                    for piece in range(4):
                        sl = slice(piece * 512, (piece + 1) * 512)
                        tensor.wait_ge(s_V, V0 + piece + 1)
                        for kc in range(C):
                            mm = tensor.matmul(
                                p_qkv[0:64, sl],
                                wqkvT.ap()[:, kc * 192 + 128:kc * 192 + 192],
                                ln1T.ap()[:, kc * SEQ + piece * 512:
                                          kc * SEQ + (piece + 1) * 512],
                                start=(kc == 0), stop=(kc == C - 1))
                        mm.then_inc(s_T, 1)
                    for i in range(32):
                        jc = i % 16
                        slot = (i % 2) * 512
                        if i == 0:
                            tensor.wait_ge(s_V, V0 + 4)
                        if i == 16:
                            tensor.wait_ge(s_V, V0 + 8)
                        if i >= 2:
                            tensor.wait_ge(s_V, V0 + 8 + (i - 2) + 1)
                        if i < 16:
                            tensor.transpose(p_t2[:, slot:slot + 64],
                                             qk_fm.ap()[64:128,
                                                        jc * 128:(jc + 1) * 128],
                                             idf[64:128, 64:128]).then_inc(s_T, 1)
                        else:
                            tensor.transpose(p_t2[:, slot:slot + 64],
                                             v_fm.ap()[:, jc * 128:(jc + 1) * 128],
                                             idf[0:64, 0:64]).then_inc(s_T, 1)
                    tensor.wait_ge(s_V, V0 + 41)
                    tensor.transpose(p_sm[0:1, 0:64], tmp_c[0:64, 0:1],
                                     idf[0:64, 0:64]).then_inc(s_T, 1)
                    tensor.wait_ge(s_V, V0 + 42)
                    tensor.matmul(p_sm2[:, :], ones1[:, :], negm_row[:, :],
                                  start=True, stop=True).then_inc(s_T, 1)
                n_["T"] += 4 + 4 + 32 + 2

                @blk.vector
                def _(v):
                    for piece in range(4):
                        sl = slice(piece * 512, (piece + 1) * 512)
                        v.wait_ge(s_T, T0 + piece + 1)
                        v.tensor_copy(qk_fm.ap()[:, sl], p_qkv[:, sl])
                        v.tensor_copy(qb.ap()[:, sl], p_qkv[0:64, sl]
                                      ).then_inc(s_V, 1)
                    for piece in range(4):
                        sl = slice(piece * 512, (piece + 1) * 512)
                        v.wait_ge(s_T, T0 + 4 + piece + 1)
                        v.tensor_copy(v_fm.ap()[:, sl], p_qkv[0:64, sl]
                                      ).then_inc(s_V, 1)
                    for i in range(32):
                        jc = i % 16
                        slot = (i % 2) * 512
                        v.wait_ge(s_T, T0 + 8 + i + 1)
                        if i < 16:
                            v.tensor_scalar(kT.ap()[:, jc * 64:(jc + 1) * 64],
                                            p_t2[:, slot:slot + 64], 0.125,
                                            None, OP.mult).then_inc(s_V, 1)
                        else:
                            v.tensor_copy(
                                vT_p.ap()[:, jc * 128 + 1:jc * 128 + 128:2],
                                p_t2[:, slot:slot + 64]).then_inc(s_V, 1)
                    v.drain()
                    v.tensor_reduce(tmp_a[0:64, :], qk_fm.ap()[0:64, :], AX,
                                    OP.max, apply_absolute_value=True)
                    v.tensor_reduce(tmp_b[0:64, :], qk_fm.ap()[64:128, :], AX,
                                    OP.max, apply_absolute_value=True)
                    v.drain()
                    v.tensor_mul(tmp_c[0:64, :], tmp_a[0:64, :], tmp_b[0:64, :])
                    v.drain()
                    v.tensor_scalar(tmp_c[0:64, :], tmp_c[0:64, :], -0.125,
                                    None, OP.mult).then_inc(s_V, 1)
                    v.wait_ge(s_T, T0 + 41)
                    v.tensor_copy(negm_row[:, :], p_sm[0:1, 0:64]
                                  ).then_inc(s_V, 1)
                    v.wait_ge(s_T, T0 + 42)
                    v.tensor_copy(negm[:, :], p_sm2[:, :]).then_inc(s_V, 1)
                n_["V"] += 8 + 32 + 3

            nc.all_engine_barrier()
            if STOP == "p2":
                _finish(nc, x_tok, out_hbm, sd_x, n_)
                return nc

            heads = _attn_heads()
            T0, S0, V0, DX0 = n_["T"], n_["S"], n_["V"], n_["dx"]
            qrep_done, nd_chunk_done, nd_head_done = {}, {}, {}
            exp_pos, exp_head_done, mask_pos, pairv_pos = {}, {}, {}, {}
            tc = T0 + 2 + 2
            qrep_done[0] = T0 + 2
            qrep_done[1] = T0 + 4
            k = 0
            for (nn2, ih, h, chunks) in heads:
                for (jc, i_lo, diag) in chunks:
                    tc += len(_nd_pieces(i_lo))
                    nd_chunk_done[k] = tc
                    k += 1
                nd_head_done[nn2] = tc
                if nn2 + 2 < NHEADS:
                    tc += 2
                    qrep_done[nn2 + 2] = tc
            k = 0
            for (nn2, ih, h, chunks) in heads:
                for _ in chunks:
                    exp_pos[k] = S0 + k + 1
                    k += 1
                exp_head_done[nn2] = S0 + k
            vc = V0
            k = 0
            for (nn2, ih, h, chunks) in heads:
                if nn2 >= 1:
                    vc += 1
                    pairv_pos[nn2 - 1] = vc
                for (jc, i_lo, diag) in chunks:
                    if diag:
                        vc += 1
                        mask_pos[k] = vc
                    k += 1
            vc += 1
            pairv_pos[NHEADS - 1] = vc
            pair_dma = {m: 32 * (m // 2 + 1) for m in range(NHEADS)}
            sd_p = (sd_p0, sd_p1)
            NCHUNK = k

            with (
                nc.psum_tensor("p_qr0", [128, HALF], F32) as p_qr0,
                nc.psum_tensor("p_qr1", [128, HALF], F32) as p_qr1,
                nc.psum_tensor("p_nd0", [2, HALF], F32) as p_nd0,
                nc.psum_tensor("p_nd1", [2, HALF], F32) as p_nd1,
                nc.Block() as blk,
            ):
                p_qr = (p_qr0, p_qr1)
                p_nd = (p_nd0, p_nd1)

                def emit_qrep(tensor, m):
                    _, ihm, hm, _ = heads[m]
                    if m >= 2:
                        tensor.wait_ge(s_S, exp_head_done[m - 2])
                    for half2 in range(2):
                        tensor.matmul(
                            p_qr[m % 2][:, half2 * 512:(half2 + 1) * 512],
                            sel.ap()[:, hm * 128:(hm + 1) * 128],
                            qb.ap()[:, ihm * HALF + half2 * 512:
                                    ihm * HALF + (half2 + 1) * 512],
                            start=True, stop=True).then_inc(s_T, 1)

                @blk.tensor
                def _(tensor):
                    emit_qrep(tensor, 0)
                    emit_qrep(tensor, 1)
                    kk = 0
                    for (m, ih, h, chunks) in heads:
                        for ci, (jc, i_lo, diag) in enumerate(chunks):
                            if ci == 0 and m >= 2:
                                tensor.wait_ge(s_V, pairv_pos[m - 2])
                            if diag:
                                tensor.wait_ge(s_V, mask_pos[kk])
                            else:
                                tensor.wait_ge(s_S, exp_pos[kk])
                            par = kk % 4
                            last_ci = {}
                            for ci2, (_, il2, _) in enumerate(chunks):
                                for (p2, _, _) in _nd_pieces(il2):
                                    last_ci[p2] = ci2
                            for (p, lo, hi) in _nd_pieces(i_lo):
                                tensor.matmul(
                                    p_nd[m % 2][0:2, lo:hi],
                                    vT_p.ap()[:, jc * 128 + 2 * h:
                                              jc * 128 + 2 * h + 2],
                                    E_buf.ap()[:, par * HALF + lo:
                                               par * HALF + hi],
                                    start=(ci == 0),
                                    stop=(ci == last_ci[p])).then_inc(s_T, 1)
                            kk += 1
                        if m + 2 < NHEADS:
                            emit_qrep(tensor, m + 2)
                n_["T"] = tc

                @blk.scalar
                def _(s):
                    kk = 0
                    for (m, ih, h, chunks) in heads:
                        for (jc, i_lo, diag) in chunks:
                            par = kk % 4
                            w = qrep_done[m]
                            if kk >= 4:
                                w = max(w, nd_chunk_done[kk - 4])
                            s.wait_ge(s_T, w)
                            s.activation(
                                E_buf.ap()[:, par * HALF + i_lo:(par + 1) * HALF],
                                p_qr[m % 2][:, i_lo:HALF],
                                AF.Exp,
                                bias=negm.ap()[:, h:h + 1],
                                scale=kT.ap()[:, jc * 64 + h:jc * 64 + h + 1]
                                ).then_inc(s_S, 1)
                            kk += 1
                n_["S"] += NCHUNK

                @blk.vector
                def _(v):
                    kk = 0
                    for (m, ih, h, chunks) in heads:
                        if m >= 1:
                            mp = m - 1
                            v.wait_ge(s_T, nd_head_done[mp])
                            if mp >= 2:
                                v.wait_ge(sd_p[mp % 2], pair_dma[mp - 2])
                            v.tensor_copy(
                                pair_sb.ap()[:, (mp % 2) * HALF:
                                             (mp % 2 + 1) * HALF],
                                p_nd[mp % 2][:, :]).then_inc(s_V, 1)
                        for (jc, i_lo, diag) in chunks:
                            if diag:
                                par = kk % 4
                                v.wait_ge(s_S, exp_pos[kk])
                                v.tensor_mul(
                                    E_buf.ap()[:, par * HALF + i_lo:
                                               par * HALF + i_lo + 128],
                                    E_buf.ap()[:, par * HALF + i_lo:
                                               par * HALF + i_lo + 128],
                                    tri[:, :]).then_inc(s_V, 1)
                            kk += 1
                    mp = NHEADS - 1
                    v.wait_ge(s_T, nd_head_done[mp])
                    v.wait_ge(sd_p[mp % 2], pair_dma[mp - 2])
                    v.tensor_copy(pair_sb.ap()[:, (mp % 2) * HALF:
                                               (mp % 2 + 1) * HALF],
                                  p_nd[mp % 2][:, :]).then_inc(s_V, 1)
                n_["V"] = vc

                @blk.sync
                def _(sync):
                    for (m, ih, h, chunks) in heads:
                        sync.wait_ge(s_V, pairv_pos[m])
                        sec = (m % 2) * HALF
                        sync.dma_start(
                            out=da_sb.ap()[h:h + 1, ih * HALF:(ih + 1) * HALF],
                            in_=pair_sb.ap()[0:1, sec:sec + HALF]
                            ).then_inc(sd_p[m % 2], 16)
                        sync.dma_start(
                            out=na_sb.ap()[h:h + 1, ih * HALF:(ih + 1) * HALF],
                            in_=pair_sb.ap()[1:2, sec:sec + HALF]
                            ).then_inc(sd_p[m % 2], 16)

            nc.all_engine_barrier()
            if STOP == "p3":
                _finish(nc, x_tok, out_hbm, sd_x, n_)
                return nc

            with (
                nc.psum_tensor("p_w", [128, 1536], F32) as p_w,
                nc.Block() as blk,
            ):
                T0, V0 = n_["T"], n_["V"]

                @blk.vector
                def _(v):
                    v.wait_ge(sd_p0, 32 * (NHEADS // 2))
                    v.wait_ge(sd_p1, 32 * (NHEADS // 2))
                    v.reciprocal(dr_sb[:, :], da_sb[:, :])
                    v.drain()
                    v.tensor_mul(attn_o[:, :], na_sb[:, :], dr_sb[:, :]
                                 ).then_inc(s_V, 1)
                    for idx in range(32):
                        t, piece = divmod(idx, 2)
                        v.wait_ge(s_T, T0 + idx + 1)
                        sl = slice((idx % 3) * 512, (idx % 3) * 512 + 512)
                        xsl = slice(t * D + piece * 512, t * D + piece * 512 + 512)
                        v.tensor_add(x_tok.ap()[:, xsl], x_tok.ap()[:, xsl],
                                     p_w[:, sl]).then_inc(s_V, 1)
                n_["V"] += 33

                @blk.tensor
                def _(tensor):
                    tensor.wait_ge(s_V, V0 + 1)
                    for idx in range(32):
                        t, piece = divmod(idx, 2)
                        if idx >= 3:
                            tensor.wait_ge(s_V, V0 + 1 + (idx - 3) + 1)
                        sl = slice((idx % 3) * 512, (idx % 3) * 512 + 512)
                        tensor.matmul(p_w[:, sl],
                                      attn_o.ap()[:, t * 128:(t + 1) * 128],
                                      woutT.ap()[:, piece * 512:(piece + 1) * 512],
                                      start=True, stop=True).then_inc(s_T, 1)
                n_["T"] += 32

            nc.all_engine_barrier()
            if STOP == "p3b":
                _finish(nc, x_tok, out_hbm, sd_x, n_)
                return nc

        with nc.Block() as blk:
            V0, S0, T0 = n_["V"], n_["S"], n_["T"]

            @blk.vector
            def _(v):
                for t in range(T):
                    xt = x_tok.ap()[:, t * D:(t + 1) * D]
                    v.drain()
                    v.reduce_sum(tmp_a[:, :], xt, AX)
                    v.tensor_mul(scr1[:, :], xt, xt)
                    v.drain()
                    v.reduce_sum(tmp_b[:, :], scr1[:, :], AX)
                    v.tensor_reduce(tmp_e[:, :], xt, AX, OP.max)
                    v.tensor_reduce(tmp_f[:, :], xt, AX, OP.min, negate=False)
                    v.drain()
                    v.tensor_scalar(mu_all.ap()[:, t:t + 1], tmp_a[:, :],
                                    1.0 / D, None, OP.mult)
                    v.tensor_scalar(tmp_b[:, :], tmp_b[:, :], 1.0 / D, None,
                                    OP.mult)
                    v.drain()
                    v.tensor_mul(tmp_c[:, :], mu_all.ap()[:, t:t + 1],
                                 mu_all.ap()[:, t:t + 1])
                    v.drain()
                    v.tensor_sub(tmp_d[:, :], tmp_b[:, :], tmp_c[:, :]
                                 ).then_inc(s_V, 1)
                    v.wait_ge(s_S, S0 + t + 1)
                    v.reciprocal(rstd_all.ap()[:, t:t + 1], tmp_a[:, :])
                    v.tensor_sub(tmp_e[:, :], tmp_e[:, :], mu_all.ap()[:, t:t + 1])
                    v.tensor_sub(tmp_f[:, :], mu_all.ap()[:, t:t + 1], tmp_f[:, :])
                    v.drain()
                    v.tensor_max(tmp_e[:, :], tmp_e[:, :], tmp_f[:, :])
                    v.drain()
                    v.tensor_mul(tmp_e[:, :], tmp_e[:, :],
                                 rstd_all.ap()[:, t:t + 1])
                    v.drain()
                    if t == 0:
                        v.tensor_copy(amax_run[:, :], tmp_e[:, :]).then_inc(s_V, 1)
                    else:
                        v.tensor_max(amax_run[:, :], amax_run[:, :], tmp_e[:, :]
                                     ).then_inc(s_V, 1)
            n_["V"] += 2 * T

            @blk.scalar
            def _(s):
                for t in range(T):
                    s.wait_ge(s_V, V0 + 2 * t + 1)
                    s.activation(tmp_a[:, :], tmp_d[:, :], AF.Sqrt,
                                 bias=eps_r[:, :], scale=1.0).then_inc(s_S, 1)
            n_["S"] += T

        nc.all_engine_barrier()

        with ExitStack() as fctx:
            fn = fctx.enter_context
            hpreT = fn(nc.sbuf_tensor("hpreT", [128, IC * SEQ], BF16))

            if STOP == "exch1":
                _finish(nc, x_tok, out_hbm, sd_x, n_)
                return nc

            with ExitStack() as f2:
                f2n = f2.enter_context
                xqT = f2n(nc.sbuf_tensor("xqT", [128, C * SEQ], BF16))
                wraw_g = f2n(nc.sbuf_tensor("wraw_g", [128, 2 * D], F32))
                wraw_u = f2n(nc.sbuf_tensor("wraw_u", [128, 2 * D], F32))
                wgq = f2n(nc.sbuf_tensor("wgq", [128, D], BF16))
                wuq = f2n(nc.sbuf_tensor("wuq", [128, D], BF16))
                wgT = f2n(nc.sbuf_tensor("wgT", [128, 2 * D], BF16))
                wuT = f2n(nc.sbuf_tensor("wuT", [128, 2 * D], BF16))
                silu_sb = f2n(nc.sbuf_tensor("silu_sb", [128, 1536], BF16))
                wsc_g = f2n(nc.sbuf_tensor("wsc_g", [128, 2], F32))
                wsc_u = f2n(nc.sbuf_tensor("wsc_u", [128, 2], F32))
                amax2 = f2n(nc.sbuf_tensor("amax2", [128, 1], F32))

                with (
                    nc.psum_tensor("p_sm4b", [1, 128], F32) as p_sm4b,
                    nc.Block() as blk,
                ):
                    V0, S0, T0 = n_["V"], n_["S"], n_["T"]

                    @blk.tensor
                    def _(tensor):
                        tensor.transpose(p_sm4b[0:1, :], amax_run[:, 0:1], idf[:, :]
                                         ).then_inc(s_T, 1)
                    n_["T"] += 1

                    @blk.vector
                    def _(v):
                        v.memset(wraw_g.ap()[0:1, 0:2048], 0.0)
                        v.drain()
                        v.wait_ge(s_T, T0 + 1)
                        v.tensor_reduce(wraw_g.ap()[0:1, 0:1], p_sm4b[0:1, :],
                                        AX, OP.max).then_inc(s_V, 1)
                    n_["V"] += 1

                    @blk.scalar
                    def _(s):
                        s.wait_ge(s_V, V0 + 1)
                        s.dma_start(out=cc_in1.ap(), in_=wraw_g.ap()[0:1, 0:2048]
                                    ).then_inc(sd_w, 16)
                        s.activation(tmp_b[:, :], clipx_r[:, :], AF.Sigmoid
                                     ).then_inc(s_S, 1)
                        s.wait_ge(s_cc, 1)
                        s.dma_start(out=gmax_r[:, :],
                                    in_=cc_out1.ap()[0:1, 0:1].to_broadcast([128, 1])
                                    ).then_inc(sd_w, 16)
                    n_["S"] += 1

                    @blk.gpsimd
                    def _(g):
                        g.wait_ge(sd_w, n_["dw"] + 16)
                        g.collective_compute(
                            "AllReduce", OP.max,
                            replica_groups=[list(range(NCORES))],
                            ins=[cc_in1.ap().opt()], outs=[cc_out1.ap().opt()],
                            ).then_inc(s_cc, 1)
                    n_["dw"] += 32
                    n_["cc"] += 1

                    @blk.vector
                    def _(v):
                        v.wait_ge(sd_w, n_["dw"])
                        v.wait_ge(s_S, S0 + 1)
                        v.tensor_mul(sc_sx[:, :], tmp_b[:, :], gmax_r[:, :])
                        v.drain()
                        v.tensor_scalar(sc_sx[:, :], sc_sx[:, :], 1.0 / 127.0,
                                        None, OP.mult)
                        v.drain()
                        v.reciprocal(sc_rx[:, :], sc_sx[:, :])
                        v.drain()
                        for t in range(T):
                            v.tensor_mul(amat.ap()[:, t:t + 1],
                                         rstd_all.ap()[:, t:t + 1], sc_rx[:, :])
                            v.drain()
                            v.tensor_mul(tmp_c[:, :], mu_all.ap()[:, t:t + 1],
                                         amat.ap()[:, t:t + 1])
                            v.drain()
                            v.tensor_scalar(bmat.ap()[:, t:t + 1], tmp_c[:, :],
                                            -1.0, None, OP.mult)
                            v.drain()
                        v.sem_inc(s_V, 1)
                    n_["V"] += 1

                nc.all_engine_barrier()

                with nc.Block() as blk:
                    V0, T0 = n_["V"], n_["T"]
                    pos_q = {t: V0 + t + 1 for t in range(T)}
                    tq_after = {}
                    tqc = [0, 0]
                    for t in range(T):
                        tqc[t % 2] += 16 * C
                        tq_after[t] = tqc[t % 2]

                    @blk.vector
                    def _(v):
                        for t in range(T):
                            xt = x_tok.ap()[:, t * D:(t + 1) * D]
                            wq = wgq if t % 2 == 0 else wuq
                            if t >= 2:
                                v.wait_ge(sd_tq[t % 2], tq_after[t - 2])
                            v.drain()
                            v.tensor_scalar(scr1[:, :], xt,
                                            amat.ap()[:, t:t + 1],
                                            bmat.ap()[:, t:t + 1],
                                            OP.mult, OP.add)
                            v.drain()
                            v.tensor_scalar(scr1[:, :], scr1[:, :], MAGIC,
                                            MAGIC, OP.add, OP.subtract)
                            v.drain()
                            v.tensor_scalar(wq[:, :], scr1[:, :], 127.0,
                                            -128.0, OP.min, OP.max
                                            ).then_inc(s_V, 1)
                    n_["V"] += T

                    @blk.sync
                    def _(sync):
                        for t in range(T):
                            wq = wgq if t % 2 == 0 else wuq
                            sync.wait_ge(s_V, pos_q[t])
                            for j in range(C):
                                sync.dma_start_transpose(
                                    xqT.ap()[:, j * SEQ + t * 128:
                                             j * SEQ + (t + 1) * 128],
                                    wq.ap()[:, j * 128:(j + 1) * 128]
                                    ).then_inc(sd_tq[t % 2], 16)
                        sync.wait_ge(sd_tq[0], tqc[0])
                        sync.wait_ge(sd_tq[1], tqc[1])

                nc.all_engine_barrier()

                T0, S0, V0 = n_["T"], n_["S"], n_["V"]
                dma_g = {mi: 16 * (mi // 2 + 1) for mi in range(IC)}
                dma_u = {mi: 16 * (mi // 2 + 1) for mi in range(IC)}
                tp_done = {mi: 128 * (mi // 2 + 1) for mi in range(IC)}
                pos_qg, pos_qu = {}, {}
                pos_stt, pos_macc = {}, {}
                vcn = V0
                for step in range(IC + 2):
                    if step >= 2:
                        m = step - 2
                        for q4 in range(4):
                            vcn += 1
                            pos_stt[m, q4] = vcn
                            vcn += 1
                            pos_macc[m, q4] = vcn
                    if step < IC:
                        vcn += 1
                        pos_qg[step] = vcn
                        vcn += 1
                        pos_qu[step] = vcn
                pos_gmm, pos_umm = {}, {}
                tcn = T0
                for mi_ in range(IC):
                    for q4 in range(4):
                        tcn += 1
                        pos_gmm[mi_, q4] = tcn
                        tcn += 1
                        pos_umm[mi_, q4] = tcn
                pos_silu = {}
                scn = S0
                for mi_ in range(IC):
                    for q4 in range(4):
                        scn += 1
                        pos_silu[mi_, q4] = scn

                with (
                    nc.psum_tensor("p_g", [128, 1536], F32) as p_g,
                    nc.psum_tensor("p_u", [128, 1536], F32) as p_u,
                    nc.Block() as blk,
                ):
                    @blk.scalar
                    def _(s):
                        def dma_pair(mi):
                            if mi >= 2:
                                s.wait_ge(s_V, pos_qg[mi - 2])
                            s.dma_start(
                                out=wraw_g.ap()[:, (mi % 2) * D:(mi % 2 + 1) * D],
                                in_=wg_hbm[mi * 128:(mi + 1) * 128, :]
                                ).then_inc(sd_wg[mi % 2], 16)
                            if mi >= 2:
                                s.wait_ge(s_V, pos_qu[mi - 2])
                            s.dma_start(
                                out=wraw_u.ap()[:, (mi % 2) * D:(mi % 2 + 1) * D],
                                in_=wu_hbm[mi * 128:(mi + 1) * 128, :]
                                ).then_inc(sd_wu[mi % 2], 16)
                        dma_pair(0)
                        dma_pair(1)
                        for mi in range(IC):
                            for q4 in range(4):
                                gq = mi * 4 + q4
                                sl = slice((gq % 3) * 512, (gq % 3) * 512 + 512)
                                if gq >= 3:
                                    m2, q2 = divmod(gq - 3, 4)
                                    s.wait_ge(s_V, pos_stt[m2, q2])
                                s.wait_ge(s_T, pos_gmm[mi, q4])
                                s.activation(silu_sb[:, sl], p_g[:, sl],
                                             AF.Silu, scale=sc_sx.ap()[:, 0:1]
                                             ).then_inc(s_S, 1)
                            if mi + 2 < IC:
                                dma_pair(mi + 2)
                    n_["S"] = scn

                    @blk.sync
                    def _(sync):
                        for mi in range(IC):
                            sync.wait_ge(s_V, pos_qg[mi])
                            for j in range(C):
                                sync.dma_start_transpose(
                                    wgT.ap()[:, (mi % 2) * D + j * 128:
                                             (mi % 2) * D + (j + 1) * 128],
                                    wgq.ap()[:, j * 128:(j + 1) * 128]
                                    ).then_inc(sd_tg[mi % 2], 16)
                            sync.wait_ge(s_V, pos_qu[mi])
                            for j in range(C):
                                sync.dma_start_transpose(
                                    wuT.ap()[:, (mi % 2) * D + j * 128:
                                             (mi % 2) * D + (j + 1) * 128],
                                    wuq.ap()[:, j * 128:(j + 1) * 128]
                                    ).then_inc(sd_tu[mi % 2], 16)

                    @blk.vector
                    def _(v):
                        def prep(mi):
                            for side in ("g", "u"):
                                raw = wraw_g if side == "g" else wraw_u
                                wq = wgq if side == "g" else wuq
                                wsc = wsc_g if side == "g" else wsc_u
                                dmap = dma_g if side == "g" else dma_u
                                dsem = (sd_wg[mi % 2] if side == "g"
                                        else sd_wu[mi % 2])
                                tsem = (sd_tg[(mi - 1) % 2] if side == "g"
                                        else sd_tu[(mi - 1) % 2])
                                v.wait_ge(dsem, dmap[mi])
                                if mi >= 1:
                                    v.wait_ge(tsem, tp_done[mi - 1])
                                rsl = raw.ap()[:, (mi % 2) * D:(mi % 2 + 1) * D]
                                v.drain()
                                v.tensor_reduce(wsc.ap()[:, 0:1], rsl, AX,
                                                OP.max, apply_absolute_value=True)
                                v.drain()
                                v.tensor_scalar(wsc.ap()[:, 0:1], wsc.ap()[:, 0:1],
                                                1.0 / 127.0, None, OP.mult)
                                v.drain()
                                v.reciprocal(wsc.ap()[:, 1:2], wsc.ap()[:, 0:1])
                                v.drain()
                                v.tensor_scalar(scr1[:, :], rsl,
                                                wsc.ap()[:, 1:2], MAGIC,
                                                OP.mult, OP.add)
                                v.drain()
                                v.tensor_scalar(wq[:, :], scr1[:, :], MAGIC,
                                                wsc.ap()[:, 0:1], OP.subtract,
                                                OP.mult).then_inc(s_V, 1)

                        def epi(m):
                            for q4 in range(4):
                                gq = m * 4 + q4
                                sl = slice((gq % 3) * 512, (gq % 3) * 512 + 512)
                                v.wait_ge(s_S, pos_silu[m, q4])
                                v.wait_ge(s_T, pos_umm[m, q4])
                                hsl = slice(m * SEQ + q4 * 512,
                                            m * SEQ + (q4 + 1) * 512)
                                v.scalar_tensor_tensor(
                                    hpreT.ap()[:, hsl], p_u[:, sl],
                                    sc_sx.ap()[:, 0:1], silu_sb[:, sl],
                                    OP.mult, OP.mult).then_inc(s_V, 1)
                                v.drain()
                                v.tensor_reduce(tmp_e[:, :], hpreT.ap()[:, hsl],
                                                AX, OP.max,
                                                apply_absolute_value=True)
                                v.drain()
                                if m == 0 and q4 == 0:
                                    v.tensor_copy(amax2[:, :], tmp_e[:, :]
                                                  ).then_inc(s_V, 1)
                                else:
                                    v.tensor_max(amax2[:, :], amax2[:, :],
                                                 tmp_e[:, :]).then_inc(s_V, 1)

                        prep(0)
                        prep(1)
                        for mi in range(2, IC):
                            epi(mi - 2)
                            prep(mi)
                        epi(IC - 2)
                        epi(IC - 1)
                    n_["V"] = vcn

                    @blk.tensor
                    def _(tensor):
                        for mi in range(IC):
                            tensor.wait_ge(sd_tg[mi % 2], tp_done[mi])
                            tensor.wait_ge(sd_tu[mi % 2], tp_done[mi])
                            for q4 in range(4):
                                gq = mi * 4 + q4
                                sl = slice((gq % 3) * 512, (gq % 3) * 512 + 512)
                                if gq >= 3:
                                    m2, q2 = divmod(gq - 3, 4)
                                    tensor.wait_ge(s_S, pos_silu[m2, q2])
                                for kc in range(C):
                                    mm = tensor.matmul(
                                        p_g[:, sl],
                                        wgT.ap()[:, (mi % 2) * D + kc * 128:
                                                 (mi % 2) * D + (kc + 1) * 128],
                                        xqT.ap()[:, kc * SEQ + q4 * 512:
                                                 kc * SEQ + (q4 + 1) * 512],
                                        start=(kc == 0), stop=(kc == C - 1))
                                mm.then_inc(s_T, 1)
                                if gq >= 3:
                                    m2, q2 = divmod(gq - 3, 4)
                                    tensor.wait_ge(s_V, pos_stt[m2, q2])
                                for kc in range(C):
                                    mm = tensor.matmul(
                                        p_u[:, sl],
                                        wuT.ap()[:, (mi % 2) * D + kc * 128:
                                                 (mi % 2) * D + (kc + 1) * 128],
                                        xqT.ap()[:, kc * SEQ + q4 * 512:
                                                 kc * SEQ + (q4 + 1) * 512],
                                        start=(kc == 0), stop=(kc == C - 1))
                                mm.then_inc(s_T, 1)
                    n_["T"] = tcn

                nc.all_engine_barrier()
                if STOP == "p5":
                    _finish(nc, x_tok, out_hbm, sd_x, n_)
                    return nc

                with (
                    nc.psum_tensor("p_sm5", [1, 128], F32) as p_sm5,
                    nc.Block() as blk,
                ):
                    V0, S0, T0 = n_["V"], n_["S"], n_["T"]

                    @blk.tensor
                    def _(tensor):
                        tensor.transpose(p_sm5[0:1, :], amax2[:, 0:1], idf[:, :]
                                         ).then_inc(s_T, 1)
                    n_["T"] += 1

                    @blk.vector
                    def _(v):
                        v.memset(wraw_g.ap()[0:1, 0:2048], 0.0)
                        v.drain()
                        v.wait_ge(s_T, T0 + 1)
                        v.tensor_reduce(wraw_g.ap()[0:1, 0:1], p_sm5[0:1, :],
                                        AX, OP.max).then_inc(s_V, 1)
                    n_["V"] += 1

                    @blk.scalar
                    def _(s):
                        s.wait_ge(s_V, V0 + 1)
                        s.dma_start(out=cc_in2.ap(), in_=wraw_g.ap()[0:1, 0:2048]
                                    ).then_inc(sd_w, 16)
                        s.activation(tmp_b[:, :], clipd_r[:, :], AF.Sigmoid
                                     ).then_inc(s_S, 1)
                        s.wait_ge(s_cc, 2)
                        s.dma_start(out=gmax_r[:, :],
                                    in_=cc_out2.ap()[0:1, 0:1].to_broadcast(
                                        [128, 1])).then_inc(sd_w, 16)
                    n_["S"] += 1

                    @blk.gpsimd
                    def _(g):
                        g.wait_ge(sd_w, n_["dw"] + 16)
                        g.collective_compute(
                            "AllReduce", OP.max,
                            replica_groups=[list(range(NCORES))],
                            ins=[cc_in2.ap().opt()], outs=[cc_out2.ap().opt()],
                            ).then_inc(s_cc, 1)
                    n_["dw"] += 32
                    n_["cc"] += 1

                    @blk.vector
                    def _(v):
                        v.wait_ge(sd_w, n_["dw"])
                        v.wait_ge(s_S, S0 + 1)
                        v.tensor_mul(sc_sh[:, :], tmp_b[:, :], gmax_r[:, :])
                        v.drain()
                        v.tensor_scalar(sc_sh[:, :], sc_sh[:, :], 1.0 / 127.0,
                                        None, OP.mult)
                        v.drain()
                        v.reciprocal(sc_rh[:, :], sc_sh[:, :]).then_inc(s_V, 1)
                    n_["V"] += 1

                nc.all_engine_barrier()

                with nc.Block() as blk:
                    @blk.vector
                    def _(v):
                        for mi in range(IC):
                            for hh in range(2):
                                sl = slice(mi * SEQ + hh * 1024,
                                           mi * SEQ + (hh + 1) * 1024)
                                v.drain()
                                v.tensor_scalar(scr1[:, :], hpreT.ap()[:, sl],
                                                sc_rh.ap()[:, 0:1], MAGIC,
                                                OP.mult, OP.add)
                                v.drain()
                                v.tensor_scalar(scr1[:, :], scr1[:, :], MAGIC,
                                                127.0, OP.subtract, OP.min)
                                v.drain()
                                v.tensor_scalar(hpreT.ap()[:, sl], scr1[:, :],
                                                -128.0, None, OP.max
                                                ).then_inc(s_V, 1)
                    n_["V"] += IC * 2

                nc.all_engine_barrier()

            with ExitStack() as f3:
                f3n = f3.enter_context
                wd_rs = f3n(nc.sbuf_tensor("wd_rs", [128, C], F32))
                wd_rr = f3n(nc.sbuf_tensor("wd_rr", [128, C], F32))
                wdraw = f3n(nc.sbuf_tensor("wdraw", [128, II], F32))
                wdq = f3n(nc.sbuf_tensor("wdq", [128, 2816], BF16))
                wdT = f3n(nc.sbuf_tensor("wdT", [128, 11 * D], BF16))

                with nc.Block() as blk:
                    V0 = n_["V"]

                    @blk.scalar
                    def _(g):
                        for hc in range(C):
                            if hc >= 1:
                                g.wait_ge(s_V, V0 + hc)
                            g.dma_start(out=wdraw[:, :],
                                        in_=wd_hbm[hc * 128:(hc + 1) * 128, :]
                                        ).then_inc(sd_wd, 16)

                    @blk.vector
                    def _(v):
                        for hc in range(C):
                            v.wait_ge(sd_wd, 16 * (hc + 1))
                            v.drain()
                            v.tensor_reduce(wd_rs.ap()[:, hc:hc + 1],
                                            wdraw[:, :], AX, OP.max,
                                            apply_absolute_value=True
                                            ).then_inc(s_V, 1)
                        for hc in range(C):
                            v.drain()
                            v.tensor_scalar(wd_rs.ap()[:, hc:hc + 1],
                                            wd_rs.ap()[:, hc:hc + 1],
                                            1.0 / 127.0, None, OP.mult)
                            v.drain()
                            v.reciprocal(wd_rr.ap()[:, hc:hc + 1],
                                         wd_rs.ap()[:, hc:hc + 1])
                        v.drain()
                        v.sem_inc(s_V, 1)
                    n_["V"] += C + 1

                nc.all_engine_barrier()

                td_cnt = [0, 0]
                for half in range(2):
                    icn = 11
                    with (
                        nc.psum_tensor("p_d%d" % half, [128, 1536], F32) as p_d,
                        nc.Block() as blk,
                    ):
                        T0, S0, V0 = n_["T"], n_["S"], n_["V"]
                        WD0 = 16 * C * (half + 1)
                        pos_wq = {hc: V0 + hc + 1 for hc in range(C)}
                        V_PREP = V0 + C
                        pos_add = {idx: V_PREP + idx + 1 for idx in range(32)}
                        pos_dmm = {idx: T0 + idx + 1 for idx in range(32)}
                        td_after = {}
                        tdc = [td_cnt[0], td_cnt[1]]
                        for hc in range(C):
                            tdc[hc % 2] += 16 * icn
                            td_after[hc] = tdc[hc % 2]

                        @blk.scalar
                        def _(g):
                            for hc in range(C):
                                if hc >= 1:
                                    g.wait_ge(s_V, pos_wq[hc - 1])
                                g.dma_start(
                                    out=wdraw[:, 0:1408],
                                    in_=wd_hbm[hc * 128:(hc + 1) * 128,
                                               half * 1408:(half + 1) * 1408]
                                    ).then_inc(sd_wd, 16)

                        @blk.sync
                        def _(sync):
                            for hc in range(C):
                                sync.wait_ge(s_V, pos_wq[hc])
                                for j in range(icn):
                                    sync.dma_start_transpose(
                                        wdT.ap()[:, j * D + hc * 128:
                                                 j * D + (hc + 1) * 128],
                                        wdq.ap()[:, (hc % 2) * 1408 + j * 128:
                                                 (hc % 2) * 1408 + (j + 1) * 128]
                                        ).then_inc(sd_td[hc % 2], 16)
                            if half == 1:
                                for t in range(T):
                                    sync.wait_ge(s_V, pos_add[2 * t + 1])
                                    sync.dma_start(
                                        out=out_hbm[t * 128:(t + 1) * 128, :],
                                        in_=x_tok.ap()[:, t * D:(t + 1) * D]
                                        ).then_inc(sd_x, 16)
                                sync.wait_ge(sd_x, n_["dx"] + 16 * T)

                        @blk.vector
                        def _(v):
                            for hc in range(C):
                                v.wait_ge(sd_wd, WD0 + 16 * (hc + 1))
                                if hc >= 2:
                                    v.wait_ge(sd_td[hc % 2], td_after[hc - 2])
                                elif half == 1:
                                    v.wait_ge(sd_td[hc % 2], td_cnt[hc % 2])
                                for so in (0, 1024):
                                    w = min(1024, 1408 - so)
                                    v.drain()
                                    v.tensor_scalar(
                                        scr1[:, 0:w],
                                        wdraw.ap()[:, so:so + w],
                                        wd_rr.ap()[:, hc:hc + 1], MAGIC,
                                        OP.mult, OP.add)
                                    v.drain()
                                    v.tensor_scalar(
                                        wdq.ap()[:, (hc % 2) * 1408 + so:
                                                 (hc % 2) * 1408 + so + w],
                                        scr1[:, 0:w],
                                        MAGIC, wd_rs.ap()[:, hc:hc + 1],
                                        OP.subtract, OP.mult)
                                v.drain()
                                v.sem_inc(s_V, 1)
                            for idx in range(32):
                                t, piece = divmod(idx, 2)
                                sl = slice((idx % 3) * 512, (idx % 3) * 512 + 512)
                                xsl = slice(t * D + piece * 512,
                                            t * D + piece * 512 + 512)
                                v.wait_ge(s_T, pos_dmm[idx])
                                v.scalar_tensor_tensor(
                                    x_tok.ap()[:, xsl], p_d[:, sl],
                                    sc_sh.ap()[:, 0:1], x_tok.ap()[:, xsl],
                                    OP.mult, OP.add).then_inc(s_V, 1)
                        n_["V"] += C + 32

                        @blk.tensor
                        def _(tensor):
                            tensor.wait_ge(sd_td[0], tdc[0])
                            tensor.wait_ge(sd_td[1], tdc[1])
                            for idx in range(32):
                                t, piece = divmod(idx, 2)
                                sl = slice((idx % 3) * 512, (idx % 3) * 512 + 512)
                                if idx >= 3:
                                    tensor.wait_ge(s_V, pos_add[idx - 3])
                                for j in range(icn):
                                    ic = half * icn + j
                                    mm = tensor.matmul(
                                        p_d[:, sl],
                                        hpreT.ap()[:, ic * SEQ + t * 128:
                                                   ic * SEQ + (t + 1) * 128],
                                        wdT.ap()[:, j * D + piece * 512:
                                                 j * D + (piece + 1) * 512],
                                        start=(j == 0), stop=(j == icn - 1))
                                mm.then_inc(s_T, 1)
                        n_["T"] += 32
                        td_cnt[0], td_cnt[1] = tdc[0], tdc[1]

                    if half == 0:
                        nc.all_engine_barrier()

    return nc


def kernel(L0_attn_x, L0_attn_w_qkv, L0_attn_w_out, L0_ffn_clip_x,
           L0_ffn_clip_down, L0_ffn_w_gate, L0_ffn_w_up, L0_ffn_w_down):
    x = np.ascontiguousarray(np.asarray(L0_attn_x, dtype=np.float32))
    nc = build_nc()
    common = dict(
        w_qkv=np.ascontiguousarray(np.asarray(L0_attn_w_qkv, np.float32)),
        w_out=np.ascontiguousarray(np.asarray(L0_attn_w_out, np.float32)),
        clip_x=np.ascontiguousarray(np.asarray(L0_ffn_clip_x, np.float32)),
        clip_d=np.ascontiguousarray(np.asarray(L0_ffn_clip_down, np.float32)),
        w_gate=np.ascontiguousarray(np.asarray(L0_ffn_w_gate, np.float32)),
        w_up=np.ascontiguousarray(np.asarray(L0_ffn_w_up, np.float32)),
        w_down=np.ascontiguousarray(np.asarray(L0_ffn_w_down, np.float32)),
    )
    in_maps = [dict(common, x=np.ascontiguousarray(x[b])) for b in range(NCORES)]
    res = run_bass_kernel_spmd(nc, in_maps, core_ids=list(range(NCORES)))
    return np.stack([res.results[b]["out"] for b in range(NCORES)], axis=0)
